# version 02 (final)
# baseline (speedup 1.0000x reference)
import numpy as np
import ml_dtypes
import contextlib
import concourse.bass as bass
import concourse.mybir as mybir
from concourse.bass_utils import run_bass_kernel_spmd
from concourse.bass import create_sync_update

F32 = mybir.dt.float32
BF16 = mybir.dt.bfloat16
U8 = mybir.dt.uint8
ALU = mybir.AluOpType
ACT = mybir.ActivationFunctionType
AX = mybir.AxisListType
MAGIC = 1.5 * 2 ** 23

NC = 8
B_, S_, D = 4, 2048, 4096
T = B_ * S_
TOK = T // NC
NT = TOK // 128
NDC = D // 128
FF = 11008
FSH = FF // NC
DSH = D // NC
DIM = 512
DFF = 2048
NSUB = 4
SCANC = 64
NCH = TOK // SCANC
ENGS = ("sync", "vector", "scalar", "tensor", "gpsimd")
bfdt = ml_dtypes.bfloat16


def _f_matrix_T_perm(up):
    n = D
    t = np.arange(n)[:, None]
    F = np.zeros((n, n), np.float32)
    F[:, 0] = 1.0 / np.sqrt(n)
    k = np.arange(1, (n - 1) // 2 + 1)
    ang = 2 * np.pi * k[None, :] * t / n
    F[:, 2 * k - 1] = np.sqrt(2.0 / n) * np.cos(ang)
    F[:, 2 * k] = np.sqrt(2.0 / n) * np.sin(ang)
    F[:, n - 1] = ((-1) ** t[:, 0]).astype(np.float32) / np.sqrt(n)
    return np.ascontiguousarray(F[up, :].T)


def build(debug=False, stage=5, sub=9):
    nc = bass.Bass("TRN2", target_bir_lowering=False, debug=False)

    def inp(name, shape, dt=F32):
        return nc.dram_tensor(name, list(shape), dt, kind="ExternalInput").ap()

    x_in = inp("x", (TOK, D))
    wcTs = inp("wcTs", (2, D, DSH))
    ften = inp("ften", (D, DSH), BF16)
    c0 = inp("c0", (3, D, 32))
    m0 = inp("m0", (3, D, 32), U8)
    c1 = inp("c1", (3, D, 8))
    m1 = inp("m1", (3, D, 8), U8)
    encT_in = inp("encT", (D, 40))
    biasv = inp("biasv", (2, 128, D))
    winT_in = inp("winT", (D, DIM))
    binT_in = inp("binT", (128, DIM))
    lnab_in = inp("lnab", (NSUB, 6, 128, DIM))
    b1T_in = inp("b1T", (NSUB, 128, DFF // 128))
    phz_in = inp("phz", (NSUB, 4, 128, 4))
    lnl_in = inp("lnl", (2, 128, DIM))
    w1T_in = inp("w1T", (NSUB, DIM, DFF))
    w2T_in = inp("w2T", (NSUB, DFF, DIM))
    woutT_in = inp("woutT", (DIM, D))
    boutT_in = inp("boutT", (128, D))
    sclT_in = inp("sclT", (128, D))
    lrA_in = inp("lrA", (D, 64))
    lrB_in = inp("lrB", (64, D))
    wg_in = inp("wg", (FSH, D))
    wu_in = inp("wu", (FSH, D))
    wd_in = inp("wd", (DSH, FF))
    ident_in = inp("ident", (128, 128), BF16)
    csel_in = inp("csel", (128, 16))

    out_hbm = nc.dram_tensor("out", [TOK, D], F32, kind="ExternalOutput").ap()
    dbg = {}
    if debug:
        for name, shape in [("d_xn1", (TOK, D)), ("d_h0", (TOK, DIM)),
                            ("d_hs", (NSUB, TOK, DIM)), ("d_xq", (TOK, D)),
                            ("d_y0", (TOK, DIM)), ("d_c0", (TOK, DIM))]:
            dbg[name] = nc.dram_tensor(name, list(shape), F32, kind="ExternalOutput").ap()
        for name, shape in [("d_bT0", (128, NDC, TOK)), ("d_bT1", (128, NDC, TOK)),
                            ("d_etT", (40, NT, 128))]:
            dbg[name] = nc.dram_tensor(name, list(shape), BF16, kind="ExternalOutput").ap()

    atT_sh = nc.dram_tensor("atT_sh", [40, DSH], F32).ap()
    atT_ag = nc.dram_tensor("atT_ag", [NC, 40, DSH], F32).ap()
    wcT_sh = nc.dram_tensor("wcT_sh", [D, DSH], BF16).ap()
    wcT_ag = nc.dram_tensor("wcT_ag", [NC, D, DSH], BF16).ap()
    wgT_sh = nc.dram_tensor("wgT_sh", [D, FSH], BF16).ap()
    wgT_ag = nc.dram_tensor("wgT_ag", [NC, D, FSH], BF16).ap()
    wuT_sh = nc.dram_tensor("wuT_sh", [D, FSH], BF16).ap()
    wuT_ag = nc.dram_tensor("wuT_ag", [NC, D, FSH], BF16).ap()
    wdT_sh = nc.dram_tensor("wdT_sh", [FF, DSH], BF16).ap()
    wdT_ag = nc.dram_tensor("wdT_ag", [NC, FF, DSH], BF16).ap()
    win_bf = nc.dram_tensor("win_bf", [D, DIM], BF16).ap()
    wout_bf = nc.dram_tensor("wout_bf", [DIM, D], BF16).ap()
    hqT_hbm = nc.dram_tensor("hqT_hbm", [FF, TOK], BF16).ap()
    e_sh = [nc.dram_tensor(f"e_sh{k}", [128, 8], F32).ap() for k in range(NSUB)]
    e_ag = [nc.dram_tensor(f"e_ag{k}", [NC, 128, 8], F32).ap() for k in range(NSUB)]

    top = contextlib.ExitStack()
    with top:
        sems = {}
        cnt = {}
        for name in ["cL", "cV", "cS", "cT", "cST", "cCC", "cV2", "cT2",
                     "cLf", "cLx0", "cLx1", "cLx2", "cLx3", "cLw0", "cLw1",
                     "cLg", "cLu", "cLd0", "cLd1", "cLd2", "cLd3", "cLh",
                     "cLo0", "cLo1"]:
            sems[name] = top.enter_context(nc.semaphore(name))
            cnt[name] = 0
        pmm = top.enter_context(nc.psum_tensor("pmm", [128, 4, 512], F32))
        ptr0 = top.enter_context(nc.psum_tensor("ptr0", [128, 1024], BF16))
        ptr1 = top.enter_context(nc.psum_tensor("ptr1", [128, 1024], BF16))
        block = top.enter_context(nc.Block())

        Q = {k: [] for k in ENGS}
        LASTI = {k: None for k in ENGS}

        def run(qname, fn):
            def g(e, fn=fn, qname=qname):
                r = fn(e)
                LASTI[qname] = r if r is not None else None
            Q[qname].append(g)

        def dma(qname, ch, out_ap, in_ap, accum=False):
            tgt = cnt[ch] + 16
            cnt[ch] = tgt
            sem = sems[ch]
            if accum:
                def f(e, o=out_ap, i=in_ap):
                    e.dma_start(out=o, in_=i, accum_op=ALU.add).then_inc(sem, 16)
            else:
                def f(e, o=out_ap, i=in_ap):
                    e.dma_start(out=o, in_=i).then_inc(sem, 16)
            Q[qname].append(f)
            return tgt

        def wait(qname, ch, val):
            sem = sems[ch]

            def f(e, v=val):
                e.wait_ge(sem, v)
            Q[qname].append(f)

        def mark(qname, ch, k=1):
            tgt = cnt[ch] + k
            cnt[ch] = tgt
            sem = sems[ch]

            def f(e, kk=k, qname=qname):
                li = LASTI.get(qname)
                if qname == "tensor":
                    assert li is not None, "tensor mark without instruction"
                    li.then_inc(sem, kk)
                    LASTI[qname] = None
                else:
                    e.maybe_drain_then_inc((sem, kk))
            Q[qname].append(f)
            return tgt

        def barrier_all():
            for eng in ENGS:
                for ch, sem in sems.items():
                    if cnt[ch]:
                        wait(eng, ch, cnt[ch])

        def ag(in_dram, out_dram, store_target):
            wait("gpsimd", "cST", store_target)

            def f(g, i=in_dram, o=out_dram):
                g.collective_compute(
                    "AllGather", ALU.bypass, replica_groups=[list(range(NC))],
                    ins=[i.opt()], outs=[o.opt()]).then_inc(sems["cCC"], 1)
            Q["gpsimd"].append(f)
            cnt["cCC"] += 1
            return cnt["cCC"]

        A = contextlib.ExitStack()

        def sbt(stack, name, shape, dt):
            return stack.enter_context(nc.sbuf_tensor(name, list(shape), dt))

        with A:
            ident = sbt(A, "identsb", (128, 128), BF16)
            bigT = sbt(A, "bigT", (128, NDC, TOK), BF16)
            oevac = sbt(A, "oevac", (128, 4, 512), F32)
            lnst = sbt(A, "lnst", (128, 8), F32)
            cselb = sbt(A, "cselb", (128, 16), F32)
            q_f0 = sbt(A, "q_f0", (128, 2048), F32)
            scrA = sbt(A, "scrA", (128, 2048), F32)
            h_sb = sbt(A, "h_sb", (128, NT, DIM), BF16)
            tevac = sbt(A, "tevac", (128, 8, 128), BF16)
            identA = ident.ap()

            def transp(te, dst_ap, src_ap):
                k = src_ap.shape[0]
                return te.transpose(dst_ap, src_ap, identA[0:k, 0:k])

            def qdq_chain_f(v, in_ap, out_ap, width, pt_cols, qs, rows=128):
                nb = width // 16
                x3 = in_ap[:, 0:width].rearrange("p (a b) -> p a b", b=16)
                o3 = out_ap[:, 0:width].rearrange("p (a b) -> p a b", b=16)
                bm = qs["bm"][0:rows, 0:nb]
                rb = qs["rb"][0:rows, 0:nb]
                sbn = qs["sbn"][0:rows, 0:nb]
                at = qs["t0"][0:rows, 0:width]
                c2 = qs["t1"][0:rows, 0:width]
                ut = qs["t2"][0:rows, 0:width]
                rt = qs["t3"][0:rows, 0:width]
                rA = qs["f0"][0:rows, 0:width]
                a3 = qs["t0"][0:rows, 0:width].rearrange("p (a b) -> p a b", b=16)
                r3 = qs["t3"][0:rows, 0:width].rearrange("p (a b) -> p a b", b=16)
                rbB = rb.unsqueeze(2).broadcast_to([rows, nb, 16])
                sbnB = sbn.unsqueeze(2).broadcast_to([rows, nb, 16])
                v.drain()
                v.tensor_reduce(bm, x3, AX.X, ALU.max, apply_absolute_value=True)
                v.drain()
                v.tensor_scalar(rb, bm, 1e-30, None, ALU.max)
                v.tensor_scalar(sbn, bm, -1.0 / 6.0, None, ALU.mult)
                v.drain()
                v.reciprocal(rb, rb)
                v.drain()
                v.tensor_scalar(rb, rb, 6.0, None, ALU.mult)
                v.drain()
                v.tensor_tensor(a3, x3, rbB, ALU.mult)
                v.drain()
                v.tensor_scalar(c2, at, 2.0, -2.0, ALU.min, ALU.max)
                v.drain()
                v.tensor_tensor(ut, at, c2, ALU.add)
                v.drain()
                v.tensor_scalar(c2, ut, 6.0, -6.0, ALU.min, ALU.max)
                v.drain()
                v.tensor_tensor(at, ut, c2, ALU.add)
                v.drain()
                v.tensor_scalar(rA, at, 0.5, MAGIC, ALU.mult, ALU.add)
                v.drain()
                v.tensor_scalar(rt, rA, MAGIC, None, ALU.subtract)
                v.drain()
                v.tensor_scalar(c2, rt, 6.0, -6.0, ALU.min, ALU.max)
                v.tensor_scalar(ut, rt, 4.0, -4.0, ALU.min, ALU.max)
                v.drain()
                v.scalar_tensor_tensor(at, rt, 2.0, c2, ALU.mult, ALU.subtract)
                v.drain()
                v.scalar_tensor_tensor(rt, ut, 0.5, at, ALU.mult, ALU.subtract)
                v.drain()
                v.tensor_tensor(o3, r3, sbnB, ALU.mult)
                if pt_cols:
                    v.tensor_copy(out_ap[:, width:width + pt_cols],
                                  in_ap[:, width:width + pt_cols])

            def ln_tile(src_ap, dst_ap, scrap_ap, dimn):
                import math as _m
                def f(v, src=src_ap, scrap=scrap_ap, dimn=dimn):
                    v.tensor_scalar(scrap, src, 1.0 / dimn, None, ALU.mult, ALU.add,
                                    accum_out=lnst.ap()[:, 0:1])
                run("vector", f)
                mv = mark("vector", "cV")
                wait("scalar", "cV", mv)

                def f(s, src=src_ap, dimn=dimn):
                    s.activation(scrA.ap()[:, 0:dimn], src, ACT.Square,
                                 scale=float(1.0 / _m.sqrt(dimn)),
                                 accum_out=lnst.ap()[:, 1:2])
                run("scalar", f)
                ms_ = mark("scalar", "cS")
                wait("vector", "cS", ms_)

                def f(v):
                    mu = lnst.ap()[:, 0:1]
                    ms = lnst.ap()[:, 1:2]
                    v.tensor_tensor(lnst.ap()[:, 2:3], mu, mu, ALU.mult)
                    v.drain()
                    v.tensor_tensor(lnst.ap()[:, 3:4], ms, lnst.ap()[:, 2:3], ALU.subtract)
                    v.drain()
                    v.tensor_scalar(lnst.ap()[:, 3:4], lnst.ap()[:, 3:4], 1e-5, None, ALU.add)
                run("vector", f)
                mv2 = mark("vector", "cV")
                wait("scalar", "cV", mv2)
                run("scalar", lambda s: s.activation(lnst.ap()[:, 4:5],
                                                     lnst.ap()[:, 3:4], ACT.Sqrt))
                ms2_ = mark("scalar", "cS")
                wait("vector", "cS", ms2_)

                def f(v):
                    v.drain()
                    v.reciprocal(lnst.ap()[:, 5:6], lnst.ap()[:, 4:5])
                    v.drain()
                    v.tensor_tensor(lnst.ap()[:, 6:7], lnst.ap()[:, 0:1],
                                    lnst.ap()[:, 5:6], ALU.mult)
                    v.drain()
                    v.tensor_scalar(lnst.ap()[:, 6:7], lnst.ap()[:, 6:7], -1.0, None, ALU.mult)
                run("vector", f)
                mv3 = mark("vector", "cV")
                wait("scalar", "cV", mv3)

                def f(s, src=src_ap, dst=dst_ap):
                    s.activation(dst, src, ACT.Identity,
                                 bias=lnst.ap()[:, 6:7], scale=lnst.ap()[:, 5:6])
                run("scalar", f)
                return mark("scalar", "cS")

            def ln_tile_4096(src_ap, dst_ap):
                import math as _m
                def f(v, src=src_ap):
                    sc = q_f0.ap()[:, 0:2048]
                    v.tensor_scalar(sc, src[:, 0:2048], 1.0 / D, None, ALU.mult,
                                    ALU.add, accum_out=lnst.ap()[:, 0:1])
                    v.drain()
                    v.tensor_scalar(sc, src[:, 2048:4096], 1.0 / D, None, ALU.mult,
                                    ALU.add, accum_out=lnst.ap()[:, 1:2])
                run("vector", f)
                mv = mark("vector", "cV")
                wait("scalar", "cV", mv)

                def f(s, src=src_ap):
                    sc = scrA.ap()[:, 0:2048]
                    s.activation(sc, src[:, 0:2048], ACT.Square,
                                 scale=float(1.0 / _m.sqrt(D)),
                                 accum_out=lnst.ap()[:, 2:3])
                    s.drain()
                    s.activation(sc, src[:, 2048:4096], ACT.Square,
                                 scale=float(1.0 / _m.sqrt(D)),
                                 accum_out=lnst.ap()[:, 3:4])
                run("scalar", f)
                ms_ = mark("scalar", "cS")
                wait("vector", "cS", ms_)

                def f(v):
                    mu = lnst.ap()[:, 0:1]
                    v.tensor_tensor(mu, mu, lnst.ap()[:, 1:2], ALU.add)
                    ms = lnst.ap()[:, 2:3]
                    v.tensor_tensor(ms, ms, lnst.ap()[:, 3:4], ALU.add)
                    v.drain()
                    v.tensor_tensor(lnst.ap()[:, 4:5], mu, mu, ALU.mult)
                    v.drain()
                    v.tensor_tensor(ms, ms, lnst.ap()[:, 4:5], ALU.subtract)
                    v.drain()
                    v.tensor_scalar(ms, ms, 1e-5, None, ALU.add)
                run("vector", f)
                mv2 = mark("vector", "cV")
                wait("scalar", "cV", mv2)
                run("scalar", lambda s: s.activation(lnst.ap()[:, 4:5],
                                                     lnst.ap()[:, 2:3], ACT.Sqrt))
                ms2_ = mark("scalar", "cS")
                wait("vector", "cS", ms2_)

                def f(v):
                    v.drain()
                    v.reciprocal(lnst.ap()[:, 5:6], lnst.ap()[:, 4:5])
                    v.drain()
                    v.tensor_tensor(lnst.ap()[:, 6:7], lnst.ap()[:, 0:1],
                                    lnst.ap()[:, 5:6], ALU.mult)
                    v.drain()
                    v.tensor_scalar(lnst.ap()[:, 6:7], lnst.ap()[:, 6:7], -1.0, None, ALU.mult)
                run("vector", f)
                mv3 = mark("vector", "cV")
                wait("scalar", "cV", mv3)

                def f(s, src=src_ap, dst=dst_ap):
                    s.activation(dst, src, ACT.Identity,
                                 bias=lnst.ap()[:, 6:7], scale=lnst.ap()[:, 5:6])
                run("scalar", f)
                return mark("scalar", "cS")

            def load_cast(src_dram_ap, dst_sb_ap, tail):
                ne = 1
                for s_ in tail:
                    ne *= s_
                bounce = q_f0.ap()[:, 0:ne]
                if len(tail) == 2:
                    bounce = bounce.rearrange("p (c r) -> p c r", r=tail[1])
                w = dma("sync", "cL", bounce, src_dram_ap)
                wait("vector", "cL", w)
                run("vector", lambda v, d=dst_sb_ap, b=bounce: v.tensor_copy(d, b))
                mv = mark("vector", "cV")
                wait("sync", "cV", mv)
                return mv

            dma("sync", "cL", ident[:], ident_in)
            dma("sync", "cL", cselb[:], csel_in)

            B2 = contextlib.ExitStack()
            with B2:
                B1 = contextlib.ExitStack()
                with B1:
                    biasc = sbt(B1, "biasc", (128, D), BF16)
                    encb = sbt(B1, "encb", (128, NDC, 40), BF16)
                    at_sb = sbt(B1, "at_sb", (40, NC, DSH), BF16)
                    etT = sbt(B1, "etT", (40, NT, 128), BF16)
                    wsA = sbt(B1, "wsA", (128, NDC, 40), BF16)
                    bnc = sbt(B1, "bnc", (128, 2048), BF16)

                    for half in range(2):
                        w1 = dma("sync", "cL", q_f0.ap()[:, 0:2048],
                                 biasv[0][:, 2048 * half:2048 * (half + 1)])
                        wait("vector", "cL", w1)
                        run("vector", lambda v, half=half: v.tensor_copy(
                            biasc.ap()[:, 2048 * half:2048 * (half + 1)],
                            q_f0.ap()[:, 0:2048]))
                        mv = mark("vector", "cV")
                        wait("sync", "cV", mv)
                        w2 = dma("sync", "cL", q_f0.ap()[:, 0:2048],
                                 biasv[1][:, 2048 * half:2048 * (half + 1)])
                        wait("vector", "cL", w2)

                        def f(v, half=half):
                            bs = biasc.ap()[:, 2048 * half:2048 * (half + 1)]
                            v.tensor_tensor(bs, bs, q_f0.ap()[:, 0:2048], ALU.add)
                        run("vector", f)
                        mv = mark("vector", "cV")
                        wait("sync", "cV", mv)
                    load_cast(encT_in.rearrange("(c p) r -> p c r", p=128),
                              encb.ap(), (NDC, 40))
                    for half in range(2):
                        for sc_i in range(4):
                            w1 = dma(
                                "sync", "cL",
                                q_f0.ap()[:, 0:2048].rearrange("p (c d) -> p c d", d=512),
                                winT_in[2048 * half:2048 * (half + 1), :].rearrange(
                                    "(c p) d -> p c d", p=128)[:, 4 * sc_i:4 * (sc_i + 1), :])
                            wait("vector", "cL", w1)
                            run("vector", lambda v: v.tensor_copy(
                                bnc.ap()[:, 0:2048], q_f0.ap()[:, 0:2048]))
                            mv = mark("vector", "cV")
                            wait("gpsimd", "cV", mv)
                            st = dma("gpsimd", "cST",
                                     win_bf[2048 * half:2048 * (half + 1), :].rearrange(
                                         "(c p) d -> p c d", p=128)[:, 4 * sc_i:4 * (sc_i + 1), :],
                                     bnc.ap()[:, 0:2048].rearrange("p (c d) -> p c d", d=512))
                            wait("sync", "cST", st)
                    for ci in range(4):
                        for half in range(2):
                            w1 = dma("sync", "cL", q_f0.ap()[:, 0:2048],
                                     woutT_in[128 * ci:128 * (ci + 1),
                                              2048 * half:2048 * (half + 1)])
                            wait("vector", "cL", w1)
                            run("vector", lambda v: v.tensor_copy(
                                bnc.ap()[:, 0:2048], q_f0.ap()[:, 0:2048]))
                            mv = mark("vector", "cV")
                            wait("gpsimd", "cV", mv)
                            st = dma("gpsimd", "cST",
                                     wout_bf[128 * ci:128 * (ci + 1),
                                             2048 * half:2048 * (half + 1)],
                                     bnc.ap()[:, 0:2048])
                            wait("sync", "cST", st)

                    if sub >= 1:
                        with contextlib.ExitStack() as P1s:
                            wsF = sbt(P1s, "wsF", (128, NDC, 40), F32)
                            mu8 = sbt(P1s, "mu8", (128, NDC, 32), U8)
                            fslab = sbt(P1s, "fslab", (128, NDC, 512), BF16)
                            run("vector", lambda v: v.memset(wsF.ap(), 0.0))
                            for (cc, mm, rr, roff) in [(c0, m0, 32, 0), (c1, m1, 8, 32)]:
                                for e in range(3):
                                    wv = dma("sync", "cL",
                                             q_f0.ap()[:, 0:NDC * rr].rearrange(
                                                 "p (c r) -> p c r", r=rr),
                                             cc[e].rearrange("(c p) r -> p c r", p=128))
                                    wm = dma("sync", "cL", mu8.ap()[:, :, 0:rr],
                                             mm[e].rearrange("(c p) r -> p c r", p=128))
                                    wait("vector", "cL", max(wv, wm))

                                    def f(v, rr=rr, roff=roff):
                                        cs = q_f0.ap()[:, 0:NDC * rr].rearrange(
                                            "p (c r) -> p c r", r=rr)
                                        mf = bnc.ap()[:, 0:NDC * rr].rearrange(
                                            "p (c r) -> p c r", r=rr)
                                        v.tensor_copy(mf, mu8.ap()[:, :, 0:rr])
                                        v.drain()
                                        v.tensor_tensor(mf, mf, cs, ALU.mult)
                                        v.drain()
                                        v.tensor_tensor(wsF.ap()[:, :, roff:roff + rr],
                                                        wsF.ap()[:, :, roff:roff + rr],
                                                        mf, ALU.add)
                                    run("vector", f)
                                    mv = mark("vector", "cV")
                                    wait("sync", "cV", mv)
                            run("vector", lambda v: v.tensor_scalar(
                                wsA.ap(), wsF.ap(), 0.5, None, ALU.mult))
                            mv_ws = mark("vector", "cV")
                            w_ft = dma("sync", "cLf", fslab.ap(),
                                       ften.rearrange("(c p) d -> p c d", p=128))
                            wait("tensor", "cV", mv_ws)
                            wait("tensor", "cLf", w_ft)

                            def f(te):
                                for kc in range(NDC):
                                    _li = te.matmul(pmm.ap()[0:40, 0, :], wsA.ap()[:, kc, :],
                                              fslab.ap()[:, kc, :],
                                              start=(kc == 0), stop=(kc == NDC - 1))
                                return _li
                            run("tensor", f)
                            mt = mark("tensor", "cT")
                            wait("vector", "cT", mt)
                            run("vector", lambda v: v.tensor_copy(
                                q_f0.ap()[0:40, 0:512], pmm.ap()[0:40, 0, :]))
                            mv = mark("vector", "cV")
                            wait("gpsimd", "cV", mv)
                            st_at = dma("gpsimd", "cST", atT_sh, q_f0.ap()[0:40, 0:512])
                            cc_at = ag(atT_sh, atT_ag, st_at)
                            wait("sync", "cCC", cc_at)
                            for r in range(NC):
                                w_rb = dma("sync", "cL", q_f0.ap()[0:40, 0:512], atT_ag[r])
                                wait("vector", "cL", w_rb)
                                run("vector", lambda v, r=r: v.tensor_copy(
                                    at_sb.ap()[:, r, :], q_f0.ap()[0:40, 0:512]))
                                mv = mark("vector", "cV")
                                wait("sync", "cV", mv)

                    if sub >= 2:
                        with contextlib.ExitStack() as Cs:
                            barrier_all()
                            wqin = sbt(Cs, "wqin", (128, 4096), F32)
                            wqdq = sbt(Cs, "wqdq", (128, 4096), BF16)
                            q_bm = sbt(Cs, "q_bm", (128, 3, 256), F32)
                            q_t0 = sbt(Cs, "q_t0", (128, 2048), BF16)
                            q_t1 = sbt(Cs, "q_t1", (128, 2048), BF16)
                            q_t2 = sbt(Cs, "q_t2", (128, 2048), BF16)
                            q_t3 = sbt(Cs, "q_t3", (128, 2048), BF16)
                            qs = None

                            def mkqs():
                                return {"bm": q_bm.ap()[:, 0, :], "rb": q_bm.ap()[:, 1, :],
                                        "sbn": q_bm.ap()[:, 2, :], "t0": q_t0.ap(),
                                        "t1": q_t1.ap(), "t2": q_t2.ap(), "t3": q_t3.ap(),
                                        "f0": q_f0.ap()}

                            for rc in range(4):
                                v3 = wqin.ap().rearrange("p (c d) -> p c d", d=512)
                                o3 = wqdq.ap().rearrange("p (c d) -> p c d", d=512)
                                src0 = wcTs[0][1024 * rc:1024 * (rc + 1), :].rearrange(
                                    "(c p) d -> p c d", p=128)
                                src1 = wcTs[1][1024 * rc:1024 * (rc + 1), :].rearrange(
                                    "(c p) d -> p c d", p=128)
                                wa = dma("sync", "cL", v3, src0)
                                wait("vector", "cL", wa)
                                run("vector", lambda v, o3=o3, v3=v3: v.tensor_copy(o3, v3))
                                mv = mark("vector", "cV")
                                wait("sync", "cV", mv)
                                wb = dma("sync", "cL", v3, src1)
                                wait("vector", "cL", wb)
                                run("vector", lambda v, o3=o3, v3=v3: v.tensor_tensor(
                                    o3, o3, v3, ALU.add))
                                mv = mark("vector", "cV")
                                wait("gpsimd", "cV", mv)
                                st = dma("gpsimd", "cST",
                                         wcT_sh[1024 * rc:1024 * (rc + 1), :].rearrange(
                                             "(c p) d -> p c d", p=128),
                                         o3)
                                wait("vector", "cST", st)
                                wait("sync", "cV", mv)
                            cc_wc = ag(wcT_sh, wcT_ag, cnt["cST"])

                            import os as _os2
                            _kwq = int(_os2.environ.get("KWQ", "3"))

                            def emit_wqdq(w_in_ap, nrows, ncols, outT_dram, pt_last):
                                ntile = (nrows + 127) // 128
                                ncc = (ncols + 4095) // 4096
                                for ti in range(ntile):
                                    r0 = ti * 128
                                    rows = min(128, nrows - r0)
                                    for cc_i in range(ncc):
                                        c0_ = cc_i * 4096
                                        cw = min(4096, ncols - c0_)
                                        wait("sync", "cV", cnt["cV"])
                                        wld = dma("sync", "cL", wqin.ap()[0:rows, 0:cw],
                                                  w_in_ap[r0:r0 + rows, c0_:c0_ + cw])
                                        wait("vector", "cL", wld)
                                        qoff = 0
                                        while qoff < cw:
                                            qw = min(2048, cw - qoff)
                                            last_g = (cc_i == ncc - 1) and (qoff + qw == cw)
                                            pt = pt_last if last_g else 0

                                            def f(v, rows=rows, qoff=qoff, qw=qw, pt=pt):
                                                qdq_chain_f(
                                                    v, wqin.ap()[0:rows, qoff:qoff + qw],
                                                    wqdq.ap()[0:rows, qoff:qoff + qw],
                                                    qw - pt, pt, mkqs(), rows=rows)
                                            run("vector", f)
                                            qoff += qw
                                        mvq = mark("vector", "cV")
                                        wait("tensor", "cV", mvq)
                                        nb128 = (cw + 127) // 128
                                        for bi in range(nb128):
                                            bw = min(128, cw - bi * 128)
                                            pp = [ptr0, ptr1][bi % 2]
                                            if bi >= 2:
                                                wait("tensor", "cV2", cnt["cV2"] - 1)

                                            def f(te, rows=rows, bi=bi, bw=bw, pp=pp):
                                                _li = transp(te, pp.ap()[0:bw, 0:rows],
                                                       wqdq.ap()[0:rows,
                                                                 bi * 128:bi * 128 + bw])
                                                return _li
                                            run("tensor", f)
                                            mt = mark("tensor", "cT2")
                                            wait("vector", "cT2", mt)

                                            def f(v, bi=bi, bw=bw, rows=rows, pp=pp):
                                                v.tensor_copy(
                                                    tevac.ap()[0:bw, bi % 8, 0:rows],
                                                    pp.ap()[0:bw, 0:rows])
                                            run("vector", f)
                                            mv = mark("vector", "cV2")
                                            wait("gpsimd", "cV2", mv)
                                            dma("gpsimd", "cST",
                                                outT_dram[c0_ + bi * 128:c0_ + bi * 128 + bw,
                                                          r0:r0 + rows],
                                                tevac.ap()[0:bw, bi % 8, 0:rows])
                                            if bi % 8 == 7:
                                                wait("vector", "cST", cnt["cST"])
                                        wait("vector", "cST", cnt["cST"])

                            import os as _os
                            _kp3 = int(_os.environ.get("KP3", "3"))
                            cc_wg = cc_wu = cc_wd = cnt["cCC"]
                            if _kp3 >= 1:
                                emit_wqdq(wg_in, FSH, D, wgT_sh, 0)
                                if _kwq >= 3:
                                    cc_wg = ag(wgT_sh, wgT_ag, cnt["cST"])
                            if _kp3 >= 2:
                                emit_wqdq(wu_in, FSH, D, wuT_sh, 0)
                                cc_wu = ag(wuT_sh, wuT_ag, cnt["cST"])
                            if _kp3 >= 3:
                                emit_wqdq(wd_in, DSH, FF, wdT_sh, 128)
                                cc_wd = ag(wdT_sh, wdT_ag, cnt["cST"])

                    if sub >= 3:
                        with contextlib.ExitStack() as Ds:
                            barrier_all()
                            x1row = sbt(Ds, "x1row", (128, 2, D), F32)
                            xnrow = sbt(Ds, "xnrow", (128, 2, D), BF16)
                            p4_ms2 = {}
                            p4_tp = {}
                            for t in range(NT):
                                buf = t % 2
                                xch = "cLx0" if buf == 0 else "cLx1"
                                if t >= 2:
                                    wait("sync", "cS", p4_ms2[t - 2])
                                    wait("sync", "cT2", p4_tp[t - 2])
                                wx = dma("sync", xch, x1row.ap()[:, buf, :],
                                         x_in[128 * t:128 * (t + 1), :])
                                wait("vector", xch, wx)
                                ms2 = ln_tile_4096(x1row.ap()[:, buf, :], xnrow.ap()[:, buf, :])
                                p4_ms2[t] = ms2
                                wait("tensor", "cS", ms2)
                                for dc in range(NDC):
                                    pp = [ptr0, ptr1][dc % 2]
                                    if dc >= 2:
                                        wait("tensor", "cV2", cnt["cV2"] - 1)

                                    def f(te, buf=buf, dc=dc, pp=pp):
                                        _li = transp(te, pp.ap()[:, 0:128],
                                               xnrow.ap()[:, buf, 128 * dc:128 * (dc + 1)])
                                        return _li
                                    run("tensor", f)
                                    mt = mark("tensor", "cT2")
                                    wait("vector", "cT2", mt)

                                    def f(v, t=t, dc=dc, pp=pp):
                                        v.tensor_copy(bigT.ap()[:, dc, 128 * t:128 * (t + 1)],
                                                      pp.ap()[:, 0:128])
                                    run("vector", f)
                                    mark("vector", "cV2")
                                mv_xt = mark("vector", "cV2")
                                wait("tensor", "cV2", mv_xt)

                                def f(te, t=t):
                                    for kc in range(NDC):
                                        _li = te.matmul(pmm.ap()[:, 2, 0:40],
                                                  bigT.ap()[:, kc, 128 * t:128 * (t + 1)],
                                                  encb.ap()[:, kc, :],
                                                  start=(kc == 0), stop=(kc == NDC - 1))
                                    return _li
                                run("tensor", f)
                                mt_e = mark("tensor", "cT")
                                wait("vector", "cT", mt_e)
                                run("vector", lambda v: v.tensor_copy(
                                    tevac.ap()[:, 0, 0:40], pmm.ap()[:, 2, 0:40]))
                                mv_e = mark("vector", "cV")
                                wait("tensor", "cV", mv_e)
                                run("tensor", lambda te: transp(
                                    te, ptr0.ap()[0:40, 0:128], tevac.ap()[:, 0, 0:40]))
                                mt_eT = mark("tensor", "cT2")
                                wait("vector", "cT2", mt_eT)
                                run("vector", lambda v, t=t: v.tensor_copy(
                                    etT.ap()[:, t, :], ptr0.ap()[0:40, 0:128]))
                                mark("vector", "cV2")
                                p4_tp[t] = cnt["cT2"]
                            mv_p4 = mark("vector", "cV2")

                    if sub >= 4:
                        with contextlib.ExitStack() as Es:
                            barrier_all()
                            p5_lastuse = {}
                            wslab = sbt(Es, "wslab", (128, 2, NDC, 512), BF16)
                            wait("sync", "cCC", cc_wc)
                            wait("tensor", "cV2", mv_p4)
                            if debug:
                                wait("gpsimd", "cV2", mv_p4)
                                dma("gpsimd", "cST", dbg["d_bT0"], bigT.ap())
                                dma("gpsimd", "cST", dbg["d_etT"], etT.ap())
                            for oc in range(NC):
                                buf = oc % 2
                                if oc >= 2:
                                    wait("sync", "cT", cnt["cT"])
                                wl = 0
                                for i in range(NDC):
                                    wl = dma("sync", "cLg", wslab.ap()[:, buf, i, :],
                                             wcT_ag[oc, 128 * i:128 * (i + 1), :])
                                wait("tensor", "cLg", wl)
                                for t in range(NT):
                                    pslot = t % 2
                                    if oc * NT + t >= 2:
                                        wait("tensor", "cV", cnt["cV"])

                                    def f(te, oc=oc, t=t, buf=buf, pslot=pslot):
                                        for kc in range(NDC):
                                            _li = te.matmul(pmm.ap()[:, pslot, :],
                                                      bigT.ap()[:, kc, 128 * t:128 * (t + 1)],
                                                      wslab.ap()[:, buf, kc, :],
                                                      start=(kc == 0), stop=False)
                                        _li = te.matmul(pmm.ap()[:, pslot, :], etT.ap()[:, t, :],
                                                  at_sb.ap()[:, oc, :], start=False, stop=True)
                                        return _li
                                    run("tensor", f)
                                    mt = mark("tensor", "cT")
                                    xch2 = "cLx2" if t % 2 == 0 else "cLx3"
                                    if (t % 2,) in p5_lastuse:
                                        wait("sync", "cV", p5_lastuse[(t % 2,)])
                                    wx = dma("sync", xch2, oevac.ap()[:, 2 + t % 2, :],
                                             x_in[128 * t:128 * (t + 1),
                                                  512 * oc:512 * (oc + 1)])
                                    wait("vector", "cT", mt)
                                    wait("vector", xch2, wx)
                                    wait("vector", "cST", cnt["cST"])

                                    def f(v, oc=oc, t=t, pslot=pslot):
                                        v.tensor_tensor(oevac.ap()[:, pslot, :],
                                                        pmm.ap()[:, pslot, :],
                                                        biasc.ap()[:, 512 * oc:512 * (oc + 1)],
                                                        ALU.add)
                                        v.drain()
                                        v.tensor_tensor(oevac.ap()[:, pslot, :],
                                                        oevac.ap()[:, pslot, :],
                                                        oevac.ap()[:, 2 + t % 2, :], ALU.add)
                                    run("vector", f)
                                    mv = mark("vector", "cV")
                                    p5_lastuse[(t % 2,)] = mv
                                    wait("gpsimd", "cV", mv)
                                    dma("gpsimd", "cST",
                                        out_hbm[128 * t:128 * (t + 1), 512 * oc:512 * (oc + 1)],
                                        oevac.ap()[:, pslot, :])
                            st_x1 = cnt["cST"]
                            if debug:
                                wait("gpsimd", "cT", cnt["cT"])
                                dma("gpsimd", "cST", dbg["d_bT1"], bigT.ap())
                                st_x1 = cnt["cST"]

                if stage >= 2:
                    with contextlib.ExitStack() as Fs:
                        barrier_all()
                        x1r6 = sbt(Fs, "x1row6", (128, D), F32)
                        xnr6 = sbt(Fs, "xnrow6", (128, D), BF16)
                        xqr6 = sbt(Fs, "xqr6", (128, D), BF16)
                        xTsl = sbt(Fs, "xTsl", (128, NDC, 128), BF16)
                        winb = sbt(Fs, "winb", (128, 2, 512), BF16)
                        binb = sbt(Fs, "binb", (128, DIM), F32)
                        lrAb = sbt(Fs, "lrAb", (128, NDC, 64), BF16)
                        lrBb = sbt(Fs, "lrBb", (64, D), BF16)
                        t1_sb = sbt(Fs, "t1_sb", (128, NT, 64), BF16)
                        t1T = sbt(Fs, "t1T", (64, NT, 128), BF16)
                        q_bm6v = sbt(Fs, "q_bm6", (128, 3, 256), F32)
                        q_t06v = sbt(Fs, "q_t06", (128, 2048), BF16)
                        q_t16v = sbt(Fs, "q_t16", (128, 2048), BF16)
                        q_t26v = sbt(Fs, "q_t26", (128, 2048), BF16)
                        q_t36v = sbt(Fs, "q_t36", (128, 2048), BF16)

                        def mkqs6():
                            return {"bm": q_bm6v.ap()[:, 0, :], "rb": q_bm6v.ap()[:, 1, :],
                                    "sbn": q_bm6v.ap()[:, 2, :], "t0": q_t06v.ap(),
                                    "t1": q_t16v.ap(), "t2": q_t26v.ap(), "t3": q_t36v.ap(),
                                    "f0": q_f0.ap()}

                        w_bin = dma("sync", "cL", binb[:], binT_in)
                        load_cast(lrA_in.rearrange("(c p) r -> p c r", p=128),
                                  lrAb.ap(), (NDC, 64))
                        for half in range(2):
                            w1 = dma("sync", "cL", q_f0.ap()[:, 0:2048][0:64, :],
                                     lrB_in[:, 2048 * half:2048 * (half + 1)])
                            wait("vector", "cL", w1)
                            run("vector", lambda v, half=half: v.tensor_copy(
                                lrBb.ap()[:, 2048 * half:2048 * (half + 1)],
                                q_f0.ap()[0:64, 0:2048]))
                            mv = mark("vector", "cV")
                            wait("sync", "cV", mv)
                        wait("sync", "cST", st_x1)
                        p6_gate = {}
                        for t in range(NT):
                            xch = "cLx0" if t % 2 == 0 else "cLx1"
                            if t >= 1:
                                wait("sync", "cS", p6_gate[t - 1][0])
                                wait("sync", "cT2", p6_gate[t - 1][1])
                                wait("sync", "cV", p6_gate[t - 1][2])
                            wx = dma("sync", xch, x1r6.ap(), out_hbm[128 * t:128 * (t + 1), :])
                            wait("vector", xch, wx)
                            ms2 = ln_tile_4096(x1r6.ap(), xnr6.ap())
                            if debug:
                                wait("gpsimd", "cS", ms2)
                                dma("gpsimd", "cST", dbg["d_xn1"][128 * t:128 * (t + 1), :],
                                    xnr6.ap())
                            wait("vector", "cS", ms2)

                            def f(v):
                                qdq_chain_f(v, xnr6.ap()[:, 0:2048], xqr6.ap()[:, 0:2048],
                                            2048, 0, mkqs6())
                                qdq_chain_f(v, xnr6.ap()[:, 2048:4096],
                                            xqr6.ap()[:, 2048:4096], 1920, 128, mkqs6())
                            run("vector", f)
                            mvq = mark("vector", "cV")
                            if debug:
                                wait("gpsimd", "cV", mvq)
                                dma("gpsimd", "cST", dbg["d_xq"][128 * t:128 * (t + 1), :],
                                    xqr6.ap())
                            wait("tensor", "cV", mvq)
                            for dc in range(NDC):
                                pp = [ptr0, ptr1][dc % 2]
                                if dc >= 2:
                                    wait("tensor", "cV2", cnt["cV2"] - 1)

                                def f(te, dc=dc, pp=pp):
                                    _li = transp(te, pp.ap()[:, 0:128],
                                           xqr6.ap()[:, 128 * dc:128 * (dc + 1)])
                                    _li = transp(te, pp.ap()[:, 128:256],
                                           xnr6.ap()[:, 128 * dc:128 * (dc + 1)])
                                    return _li
                                run("tensor", f)
                                mt = mark("tensor", "cT2")
                                wait("vector", "cT2", mt)

                                def f(v, t=t, dc=dc, pp=pp):
                                    v.tensor_copy(bigT.ap()[:, dc, 128 * t:128 * (t + 1)],
                                                  pp.ap()[:, 0:128])
                                    v.tensor_copy(xTsl.ap()[:, dc, :], pp.ap()[:, 128:256])
                                run("vector", f)
                                mark("vector", "cV2")
                            mv_xt = mark("vector", "cV2")
                            wait("tensor", "cV2", mv_xt)
                            if t == 0:
                                wait("tensor", "cL", cnt["cL"])
                            h0_marks = {}
                            for kc in range(NDC):
                                wbuf = kc % 2
                                wch = "cLw0" if wbuf == 0 else "cLw1"
                                if kc >= 2:
                                    wait("sync", "cT", h0_marks[kc - 2])
                                wwin = dma("sync", wch, winb.ap()[:, wbuf, :],
                                           win_bf[128 * kc:128 * (kc + 1), :])
                                wait("tensor", wch, wwin)

                                def f(te, kc=kc, wbuf=wbuf):
                                    _li = te.matmul(pmm.ap()[:, 2, :], xTsl.ap()[:, kc, :],
                                              winb.ap()[:, wbuf, :],
                                              start=(kc == 0), stop=(kc == NDC - 1))
                                    return _li
                                run("tensor", f)
                                h0_marks[kc] = mark("tensor", "cT")

                            def f(te):
                                for kc in range(NDC):
                                    _li = te.matmul(pmm.ap()[:, 3, 0:64], xTsl.ap()[:, kc, :],
                                              lrAb.ap()[:, kc, :],
                                              start=(kc == 0), stop=(kc == NDC - 1))
                                return _li
                            run("tensor", f)
                            mt_h = mark("tensor", "cT")
                            wait("vector", "cT", mt_h)

                            def f(v, t=t):
                                v.tensor_tensor(h_sb.ap()[:, t, :], pmm.ap()[:, 2, :],
                                                binb.ap(), ALU.add)
                                v.tensor_copy(t1_sb.ap()[:, t, :], pmm.ap()[:, 3, 0:64])
                            run("vector", f)
                            p6_gate[t] = (cnt["cS"], cnt["cT2"], mark("vector", "cV"))
                        mv_p6 = mark("vector", "cV")
                        if debug:
                            wait("gpsimd", "cV", mv_p6)
                            for t in range(NT):
                                dma("gpsimd", "cST", dbg["d_h0"][128 * t:128 * (t + 1), :],
                                    h_sb.ap()[:, t, :])

                        wait("tensor", "cV", mv_p6)
                        for t in range(NT):
                            pp = [ptr0, ptr1][t % 2]
                            if t >= 2:
                                wait("tensor", "cV2", cnt["cV2"] - 1)
                            run("tensor", lambda te, t=t, pp=pp: transp(
                                te, pp.ap()[0:64, 0:128], t1_sb.ap()[:, t, :]))
                            mt = mark("tensor", "cT2")
                            wait("vector", "cT2", mt)
                            run("vector", lambda v, t=t, pp=pp: v.tensor_copy(
                                t1T.ap()[:, t, :], pp.ap()[0:64, 0:128]))
                            mark("vector", "cV2")
                        mv_t1T = mark("vector", "cV2")
                        wait("tensor", "cV2", mv_t1T)
                        for t in range(NT):
                            for oc in range(8):
                                pslot = (t * 8 + oc) % 2
                                if t * 8 + oc >= 2:
                                    wait("tensor", "cV", cnt["cV"])

                                def f(te, t=t, oc=oc, pslot=pslot):
                                    _li = te.matmul(pmm.ap()[:, pslot, :], t1T.ap()[:, t, :],
                                              lrBb.ap()[:, 512 * oc:512 * (oc + 1)],
                                              start=True, stop=True)
                                    return _li
                                run("tensor", f)
                                mt = mark("tensor", "cT")
                                wait("vector", "cT", mt)
                                wait("vector", "cST", cnt["cST"])
                                run("vector", lambda v, pslot=pslot: v.tensor_copy(
                                    oevac.ap()[:, pslot, :], pmm.ap()[:, pslot, :]))
                                mv = mark("vector", "cV")
                                wait("gpsimd", "cV", mv)
                                dma("gpsimd", "cST",
                                    out_hbm[128 * t:128 * (t + 1), 512 * oc:512 * (oc + 1)],
                                    oevac.ap()[:, pslot, :], accum=True)

                if stage >= 3:
                    with contextlib.ExitStack() as Gs:
                        barrier_all()
                        yrow = sbt(Gs, "yrow", (128, 2, DIM), BF16)
                        yT = sbt(Gs, "yT", (128, 4, TOK), BF16)
                        srf = sbt(Gs, "srf", (128, 4, TOK), F32)
                        sif = sbt(Gs, "sif", (128, 4, TOK), F32)
                        z44 = sbt(Gs, "z44", (128, 4, NCH), F32)
                        pvec = sbt(Gs, "pvec", (128, 4, 8), F32)
                        pw_r = sbt(Gs, "pw_r", (128, 4, SCANC), F32)
                        pw_i = sbt(Gs, "pw_i", (128, 4, SCANC), F32)
                        car_r = sbt(Gs, "car_r", (128, 4, NCH + 1), F32)
                        car_i = sbt(Gs, "car_i", (128, 4, NCH + 1), F32)
                        qkr = sbt(Gs, "qkr", (128, 4, NCH + 1), F32)
                        qki = sbt(Gs, "qki", (128, 4, NCH + 1), F32)
                        ein = sbt(Gs, "ein", (128, 8), F32)
                        eag_sb = sbt(Gs, "eag_sb", (128, NC, 8), F32)
                        cin_r = sbt(Gs, "cin_r", (128, 4), F32)
                        cin_i = sbt(Gs, "cin_i", (128, 4), F32)
                        lnab_sb = sbt(Gs, "lnab_sb", (128, 6, DIM), BF16)
                        phz_sb = sbt(Gs, "phz_sb", (128, 4, 4), F32)
                        b1_sb = sbt(Gs, "b1_sb", (128, DFF // 128), F32)
                        lnl_sb = sbt(Gs, "lnl_sb", (128, 2, DIM), BF16)
                        w1T_sb = sbt(Gs, "w1T_sb", (128, 4, DFF), BF16)
                        w2T_sb = sbt(Gs, "w2T_sb", (128, 16, DIM), BF16)
                        zTh = sbt(Gs, "zTh", (128, 16, 512), BF16)
                        woutb = sbt(Gs, "woutb", (128, 2, 512), BF16)
                        boutb = None

                        run("vector", lambda v: v.memset(z44.ap(), 0.0))

                        ctmp = sbt(Gs, "ctmp", (128, 4, 4), F32)
                        stmp = sbt(Gs, "stmp", (128, 4, 2 * NCH), F32)

                        def cmul_into(v, orr, oii, ar, ai, br, bi, t1, t2, t3, t4):
                            v.tensor_tensor(t1, ar, br, ALU.mult)
                            v.tensor_tensor(t2, ai, bi, ALU.mult)
                            v.tensor_tensor(t3, ar, bi, ALU.mult)
                            v.tensor_tensor(t4, ai, br, ALU.mult)
                            v.drain()
                            v.tensor_tensor(orr, t1, t2, ALU.subtract)
                            v.tensor_tensor(oii, t3, t4, ALU.add)
                            v.drain()

                        def cT4():
                            return (ctmp.ap()[:, :, 0], ctmp.ap()[:, :, 1],
                                    ctmp.ap()[:, :, 2], ctmp.ap()[:, :, 3])

                        for sub in range(NSUB):
                            if sub >= 1:
                                barrier_all()
                            for half_ in range(2):
                                w1 = dma("sync", "cL",
                                         q_f0.ap()[:, 0:1536].rearrange(
                                             "p (a d) -> p a d", d=DIM),
                                         lnab_in[sub].rearrange("a p d -> p a d")[
                                             :, 3 * half_:3 * (half_ + 1), :])
                                wait("vector", "cL", w1)
                                run("vector", lambda v, half_=half_: v.tensor_copy(
                                    lnab_sb.ap()[:, 3 * half_:3 * (half_ + 1), :],
                                    q_f0.ap()[:, 0:1536].rearrange(
                                        "p (a d) -> p a d", d=DIM)))
                                mv = mark("vector", "cV")
                                wait("sync", "cV", mv)
                            wp2 = dma("sync", "cL", phz_sb[:],
                                      phz_in[sub].rearrange("a p d -> p a d"))
                            wp3 = dma("sync", "cL", b1_sb[:], b1T_in[sub])
                            for half_ in range(2):
                                w4 = dma("sync", "cL",
                                         q_f0.ap()[:, 0:2048].rearrange(
                                             "p (c d) -> p c d", d=2048)[:, 0:1, :],
                                         w1T_in[sub][256 * half_:256 * half_ + 128, :]
                                         .rearrange("(c p) d -> p c d", p=128))
                                wait("vector", "cL", w4)
                                run("vector", lambda v, half_=half_: v.tensor_copy(
                                    w1T_sb.ap()[:, 2 * half_, :], q_f0.ap()[:, 0:2048]))
                                mv = mark("vector", "cV")
                                wait("sync", "cV", mv)
                                w4b = dma("sync", "cL", q_f0.ap()[:, 0:2048],
                                          w1T_in[sub][256 * half_ + 128:256 * (half_ + 1), :])
                                wait("vector", "cL", w4b)
                                run("vector", lambda v, half_=half_: v.tensor_copy(
                                    w1T_sb.ap()[:, 2 * half_ + 1, :], q_f0.ap()[:, 0:2048]))
                                mv = mark("vector", "cV")
                                wait("sync", "cV", mv)
                            for qi in range(4):
                                w6 = dma("sync", "cL",
                                         q_f0.ap()[:, 0:2048].rearrange(
                                             "p (c d) -> p c d", d=512),
                                         w2T_in[sub][512 * qi:512 * (qi + 1), :].rearrange(
                                             "(c p) d -> p c d", p=128))
                                wait("vector", "cL", w6)
                                run("vector", lambda v, qi=qi: v.tensor_copy(
                                    w2T_sb.ap()[:, 4 * qi:4 * (qi + 1), :],
                                    q_f0.ap()[:, 0:2048].rearrange("p (c d) -> p c d", d=512)))
                                mv = mark("vector", "cV")
                                wait("sync", "cV", mv)
                            wait("vector", "cL", max(wp2, wp3))

                            def f(v):
                                pr = phz_sb.ap()[:, 0, :]
                                pi = phz_sb.ap()[:, 1, :]
                                r2 = pvec.ap()[:, :, 0]
                                t1 = pvec.ap()[:, :, 1]
                                v.tensor_tensor(r2, pr, pr, ALU.mult)
                                v.tensor_tensor(t1, pi, pi, ALU.mult)
                                v.drain()
                                v.tensor_tensor(r2, r2, t1, ALU.add)
                                v.drain()
                                v.tensor_scalar(r2, r2, 1e-38, None, ALU.max)
                            run("vector", f)
                            mv = mark("vector", "cV")
                            wait("scalar", "cV", mv)
                            run("scalar", lambda s: s.activation(
                                pvec.ap()[:, :, 1], pvec.ap()[:, :, 0], ACT.Sqrt))
                            ms_ = mark("scalar", "cS")
                            wait("scalar", "cS", ms_)
                            run("scalar", lambda s: s.activation(
                                pvec.ap()[:, :, 2], pvec.ap()[:, :, 1], ACT.Tanh))
                            ms_ = mark("scalar", "cS")
                            wait("vector", "cS", ms_)

                            def f(v):
                                rr = pvec.ap()[:, :, 1]
                                th = pvec.ap()[:, :, 2]
                                rinv = pvec.ap()[:, :, 3]
                                fmul = pvec.ap()[:, :, 4]
                                v.drain()
                                v.reciprocal(rinv, rr)
                                v.drain()
                                v.tensor_tensor(fmul, th, rinv, ALU.mult)
                                v.drain()
                                v.tensor_tensor(pvec.ap()[:, :, 5], phz_sb.ap()[:, 0, :],
                                                fmul, ALU.mult)
                                v.tensor_tensor(pvec.ap()[:, :, 6], phz_sb.ap()[:, 1, :],
                                                fmul, ALU.mult)
                                v.drain()
                                v.tensor_copy(pw_r.ap()[:, :, 0:1],
                                              pvec.ap()[:, :, 5].unsqueeze(2))
                                v.tensor_copy(pw_i.ap()[:, :, 0:1],
                                              pvec.ap()[:, :, 6].unsqueeze(2))
                                v.drain()
                                m = 1
                                while m < SCANC:
                                    qr_ = pw_r.ap()[:, :, m - 1].unsqueeze(2)
                                    qi_ = pw_i.ap()[:, :, m - 1].unsqueeze(2)
                                    n = min(m, SCANC - m)
                                    ar = pw_r.ap()[:, :, 0:n]
                                    ai = pw_i.ap()[:, :, 0:n]
                                    cmul_into(v, pw_r.ap()[:, :, m:m + n],
                                              pw_i.ap()[:, :, m:m + n], ar, ai,
                                              qr_.broadcast_to([128, 4, n]),
                                              qi_.broadcast_to([128, 4, n]),
                                              srf.ap()[:, :, 0:n],
                                              srf.ap()[:, :, 64:64 + n],
                                              sif.ap()[:, :, 0:n],
                                              sif.ap()[:, :, 64:64 + n])
                                    m *= 2
                                v.memset(qkr.ap()[:, :, 0:1], 1.0)
                                v.memset(qki.ap()[:, :, 0:1], 0.0)
                                v.drain()
                                pCr = pw_r.ap()[:, :, SCANC - 1]
                                pCi = pw_i.ap()[:, :, SCANC - 1]
                                for k in range(1, NCH + 1):
                                    cmul_into(v, qkr.ap()[:, :, k], qki.ap()[:, :, k],
                                              qkr.ap()[:, :, k - 1], qki.ap()[:, :, k - 1],
                                              pCr, pCi, *cT4())
                            run("vector", f)
                            mark("vector", "cV")

                            for t in range(NT):
                                buf = t % 2
                                msl = ln_tile(h_sb.ap()[:, t, :], yrow.ap()[:, buf, :],
                                              q_f0.ap()[:, 0:DIM], DIM)
                                wait("vector", "cS", msl)

                                def f(v, buf=buf):
                                    yr = yrow.ap()[:, buf, :]
                                    v.tensor_tensor(yr, yr, lnab_sb.ap()[:, 0, :], ALU.mult)
                                    v.drain()
                                    v.tensor_tensor(yr, yr, lnab_sb.ap()[:, 1, :], ALU.add)
                                run("vector", f)
                                mvx = mark("vector", "cV")
                                wait("scalar", "cV", mvx)
                                run("scalar", lambda s, buf=buf: s.activation(
                                    yrow.ap()[:, buf, :], yrow.ap()[:, buf, :], ACT.Silu))
                                msx = mark("scalar", "cS")
                                wait("tensor", "cS", msx)
                                for g in range(4):
                                    pp = [ptr0, ptr1][g % 2]
                                    if g >= 2:
                                        wait("tensor", "cV2", cnt["cV2"] - 1)
                                    run("tensor", lambda te, buf=buf, g=g, pp=pp: transp(
                                        te, pp.ap()[:, 0:128],
                                        yrow.ap()[:, buf, 128 * g:128 * (g + 1)]))
                                    mt = mark("tensor", "cT2")
                                    wait("vector", "cT2", mt)
                                    run("vector", lambda v, t=t, g=g, pp=pp: v.tensor_copy(
                                        yT.ap()[:, g, 128 * t:128 * (t + 1)],
                                        pp.ap()[:, 0:128]))
                                    mark("vector", "cV2")
                            mv_yT = mark("vector", "cV2")
                            wait("vector", "cV2", mv_yT)

                            srf4 = srf.ap().rearrange("p g (k c) -> p g k c", c=SCANC)
                            sif4 = sif.ap().rearrange("p g (k c) -> p g k c", c=SCANC)
                            yT4 = yT.ap().rearrange("p g (k c) -> p g k c", c=SCANC)
                            prB = pvec.ap()[:, :, 5].unsqueeze(2).broadcast_to([128, 4, NCH])
                            piB = pvec.ap()[:, :, 6].unsqueeze(2).broadcast_to([128, 4, NCH])

                            def f(v):
                                t1 = car_r.ap()[:, :, 0:NCH]
                                t2 = car_i.ap()[:, :, 0:NCH]
                                t3 = stmp.ap()[:, :, 0:NCH]
                                t4 = stmp.ap()[:, :, NCH:2 * NCH]
                                for l in range(SCANC):
                                    if l == 0:
                                        spr = z44.ap()[:, :, 0:NCH]
                                        spi = z44.ap()[:, :, 0:NCH]
                                    else:
                                        spr = srf4[:, :, :, l - 1]
                                        spi = sif4[:, :, :, l - 1]
                                    v.tensor_tensor(t1, spr, prB, ALU.mult)
                                    v.tensor_tensor(t2, spi, piB, ALU.mult)
                                    v.tensor_tensor(t3, spi, prB, ALU.mult)
                                    v.tensor_tensor(t4, spr, piB, ALU.mult)
                                    v.drain()
                                    v.tensor_tensor(t1, t1, t2, ALU.subtract)
                                    v.tensor_tensor(sif4[:, :, :, l], t3, t4, ALU.add)
                                    v.drain()
                                    v.tensor_tensor(srf4[:, :, :, l], t1, yT4[:, :, :, l],
                                                    ALU.add)
                                    v.drain()
                            run("vector", f)

                            def f(v):
                                pCr = pw_r.ap()[:, :, SCANC - 1]
                                pCi = pw_i.ap()[:, :, SCANC - 1]
                                v.memset(car_r.ap()[:, :, 0:1], 0.0)
                                v.memset(car_i.ap()[:, :, 0:1], 0.0)
                                v.drain()
                                for k in range(1, NCH + 1):
                                    cmul_into(v, cin_i.ap()[:, 0:4], ein.ap()[:, 0:4],
                                              car_r.ap()[:, :, k - 1], car_i.ap()[:, :, k - 1],
                                              pCr, pCi, *cT4())
                                    v.tensor_tensor(car_r.ap()[:, :, k],
                                                    srf4[:, :, k - 1, SCANC - 1],
                                                    cin_i.ap()[:, 0:4], ALU.add)
                                    v.tensor_tensor(car_i.ap()[:, :, k],
                                                    sif4[:, :, k - 1, SCANC - 1],
                                                    ein.ap()[:, 0:4], ALU.add)
                                    v.drain()
                                lcr = phz_sb.ap()[:, 2, :]
                                lci = phz_sb.ap()[:, 3, :]
                                cmul_into(v, cin_r.ap()[:, 0:4], cin_i.ap()[:, 0:4],
                                          qkr.ap()[:, :, NCH], qki.ap()[:, :, NCH],
                                          lcr, lci, *cT4())
                                v.tensor_tensor(ein.ap()[:, 0:4], car_r.ap()[:, :, NCH],
                                                cin_r.ap()[:, 0:4], ALU.add)
                                v.tensor_tensor(ein.ap()[:, 4:8], car_i.ap()[:, :, NCH],
                                                cin_i.ap()[:, 0:4], ALU.add)
                            run("vector", f)
                            mv_E = mark("vector", "cV")
                            wait("gpsimd", "cV", mv_E)
                            st_e = dma("gpsimd", "cST", e_sh[sub], ein[:])
                            cc_e = ag(e_sh[sub], e_ag[sub], st_e)
                            wait("sync", "cCC", cc_e)
                            w_eag = dma("sync", "cL", eag_sb[:],
                                        e_ag[sub].rearrange("r p c -> p r c"))
                            wait("vector", "cL", w_eag)

                            def f(v):
                                lcr = phz_sb.ap()[:, 2, :]
                                lci = phz_sb.ap()[:, 3, :]
                                v.tensor_scalar(cin_r.ap(), lcr, cselb.ap()[:, 0:1], None,
                                                ALU.mult)
                                v.tensor_scalar(cin_i.ap(), lci, cselb.ap()[:, 0:1], None,
                                                ALU.mult)
                                t2 = ein.ap()[:, 0:4]
                                t3 = ein.ap()[:, 4:8]
                                cmul_into(v, t2, t3, qkr.ap()[:, :, NCH], qki.ap()[:, :, NCH],
                                          lcr, lci, *cT4())
                                v.tensor_scalar(t2, t2, cselb.ap()[:, 1:2], None, ALU.mult)
                                v.tensor_scalar(t3, t3, cselb.ap()[:, 1:2], None, ALU.mult)
                                v.drain()
                                v.tensor_tensor(cin_r.ap(), cin_r.ap(), t2, ALU.add)
                                v.tensor_tensor(cin_i.ap(), cin_i.ap(), t3, ALU.add)
                                v.drain()
                                for r in range(NC):
                                    v.tensor_scalar(t2, eag_sb.ap()[:, r, 0:4],
                                                    cselb.ap()[:, 2 + r:3 + r], None, ALU.mult)
                                    v.tensor_scalar(t3, eag_sb.ap()[:, r, 4:8],
                                                    cselb.ap()[:, 2 + r:3 + r], None, ALU.mult)
                                    v.drain()
                                    v.tensor_tensor(cin_r.ap(), cin_r.ap(), t2, ALU.add)
                                    v.tensor_tensor(cin_i.ap(), cin_i.ap(), t3, ALU.add)
                                    v.drain()
                                cinrB = cin_r.ap().unsqueeze(2).broadcast_to([128, 4, NCH])
                                ciniB = cin_i.ap().unsqueeze(2).broadcast_to([128, 4, NCH])
                                ta = sif.ap()[:, :, 0:NCH]
                                tb = sif.ap()[:, :, NCH:2 * NCH]
                                tc = stmp.ap()[:, :, 0:NCH]
                                td = stmp.ap()[:, :, NCH:2 * NCH]
                                v.tensor_tensor(ta, qkr.ap()[:, :, 0:NCH], cinrB, ALU.mult)
                                v.tensor_tensor(tb, qki.ap()[:, :, 0:NCH], ciniB, ALU.mult)
                                v.tensor_tensor(tc, qkr.ap()[:, :, 0:NCH], ciniB, ALU.mult)
                                v.tensor_tensor(td, qki.ap()[:, :, 0:NCH], cinrB, ALU.mult)
                                v.drain()
                                v.tensor_tensor(ta, ta, tb, ALU.subtract)
                                v.tensor_tensor(tc, tc, td, ALU.add)
                                v.drain()
                                v.tensor_tensor(car_r.ap()[:, :, 0:NCH],
                                                car_r.ap()[:, :, 0:NCH], ta, ALU.add)
                                v.tensor_tensor(car_i.ap()[:, :, 0:NCH],
                                                car_i.ap()[:, :, 0:NCH], tc, ALU.add)
                                v.drain()
                                dR = car_r.ap()[:, :, 0:NCH].unsqueeze(3).broadcast_to(
                                    [128, 4, NCH, SCANC])
                                dI = car_i.ap()[:, :, 0:NCH].unsqueeze(3).broadcast_to(
                                    [128, 4, NCH, SCANC])
                                pwR = pw_r.ap().unsqueeze(2).broadcast_to(
                                    [128, 4, NCH, SCANC])
                                pwI = pw_i.ap().unsqueeze(2).broadcast_to(
                                    [128, 4, NCH, SCANC])
                                sifv = sif.ap().rearrange("p g (k c) -> p g k c", c=SCANC)
                                v.tensor_copy(sifv, dR)
                                v.drain()
                                v.tensor_tensor(sifv, sifv, pwR, ALU.mult)
                                v.drain()
                                v.tensor_tensor(srf4, srf4, sifv, ALU.add)
                                v.drain()
                                v.tensor_copy(sifv, dI)
                                v.drain()
                                v.tensor_tensor(sifv, sifv, pwI, ALU.mult)
                                v.drain()
                                v.tensor_tensor(srf4, srf4, sifv, ALU.subtract)
                                v.drain()
                                v.tensor_copy(yT.ap(), srf.ap())
                            run("vector", f)
                            mv_conv = mark("vector", "cV")

                            wait("tensor", "cV", mv_conv)
                            for t in range(NT):
                                buf = t % 2
                                for g in range(4):
                                    pp = [ptr0, ptr1][g % 2]
                                    if t * 4 + g >= 2:
                                        wait("tensor", "cV2", cnt["cV2"] - 1)
                                    run("tensor", lambda te, t=t, g=g, pp=pp: transp(
                                        te, pp.ap()[:, 0:128],
                                        yT.ap()[:, g, 128 * t:128 * (t + 1)]))
                                    mt = mark("tensor", "cT2")
                                    wait("vector", "cT2", mt)
                                    run("vector", lambda v, buf=buf, g=g, pp=pp:
                                        v.tensor_copy(yrow.ap()[:, buf, 128 * g:128 * (g + 1)],
                                                      pp.ap()[:, 0:128]))
                                    mark("vector", "cV2")
                                mvy = mark("vector", "cV2")
                                wait("vector", "cV2", mvy)

                                def f(v, t=t, buf=buf):
                                    m = q_f0.ap()[:, 0:DIM]
                                    v.tensor_tensor(m, yrow.ap()[:, buf, :],
                                                    lnab_sb.ap()[:, 4, :], ALU.mult)
                                    v.drain()
                                    v.tensor_tensor(h_sb.ap()[:, t, :], h_sb.ap()[:, t, :],
                                                    m, ALU.add)
                                run("vector", f)
                                mark("vector", "cV")
                            mv_h1 = mark("vector", "cV")
                            if debug and sub == 0:
                                wait("gpsimd", "cV", mv_h1)
                                for t in range(NT):
                                    dma("gpsimd", "cST", dbg["d_c0"][128 * t:128 * (t + 1), :],
                                        h_sb.ap()[:, t, :])

                            for t in range(NT):
                                buf = t % 2
                                msl = ln_tile(h_sb.ap()[:, t, :], yrow.ap()[:, buf, :],
                                              q_f0.ap()[:, 0:DIM], DIM)
                                wait("vector", "cS", msl)

                                def f(v, buf=buf):
                                    yr = yrow.ap()[:, buf, :]
                                    v.tensor_tensor(yr, yr, lnab_sb.ap()[:, 2, :], ALU.mult)
                                    v.drain()
                                    v.tensor_tensor(yr, yr, lnab_sb.ap()[:, 3, :], ALU.add)
                                run("vector", f)
                                mvx = mark("vector", "cV")
                                wait("tensor", "cV", mvx)
                                for g in range(4):
                                    pp = [ptr0, ptr1][g % 2]
                                    if g >= 2:
                                        wait("tensor", "cV2", cnt["cV2"] - 1)
                                    run("tensor", lambda te, buf=buf, g=g, pp=pp: transp(
                                        te, pp.ap()[:, 0:128],
                                        yrow.ap()[:, buf, 128 * g:128 * (g + 1)]))
                                    mt = mark("tensor", "cT2")
                                    wait("vector", "cT2", mt)
                                    run("vector", lambda v, t=t, g=g, pp=pp: v.tensor_copy(
                                        yT.ap()[:, g, 128 * t:128 * (t + 1)],
                                        pp.ap()[:, 0:128]))
                                    mark("vector", "cV2")
                            mv_y2T = mark("vector", "cV2")
                            wait("tensor", "cV2", mv_y2T)

                            for th in range(2):
                                for fc in range(16):
                                    pslot = fc % 2
                                    if fc >= 2:
                                        wait("tensor", "cS", cnt["cS"] - 1)

                                    def f(te, th=th, fc=fc, pslot=pslot):
                                        for g in range(4):
                                            _li = te.matmul(pmm.ap()[:, pslot, :],
                                                      w1T_sb.ap()[:, g,
                                                                  128 * fc:128 * (fc + 1)],
                                                      yT.ap()[:, g,
                                                              512 * th:512 * (th + 1)],
                                                      start=(g == 0), stop=(g == 3))
                                        return _li
                                    run("tensor", f)
                                    mt = mark("tensor", "cT")
                                    wait("scalar", "cT", mt)
                                    run("scalar", lambda s, fc=fc, pslot=pslot: s.activation(
                                        zTh.ap()[:, fc, :], pmm.ap()[:, pslot, :], ACT.Silu,
                                        bias=b1_sb.ap()[:, fc:fc + 1]))
                                    mark("scalar", "cS")
                                ms_z = mark("scalar", "cS")
                                wait("tensor", "cS", ms_z)
                                for t4 in range(4):
                                    t = 4 * th + t4
                                    pslot = 2 + t4 % 2
                                    if t4 >= 2:
                                        wait("tensor", "cV", cnt["cV"])

                                    def f(te, t4=t4, pslot=pslot):
                                        for fc in range(16):
                                            _li = te.matmul(pmm.ap()[:, pslot, :],
                                                      zTh.ap()[:, fc,
                                                               128 * t4:128 * (t4 + 1)],
                                                      w2T_sb.ap()[:, fc, :],
                                                      start=(fc == 0), stop=(fc == 15))
                                        return _li
                                    run("tensor", f)
                                    mt = mark("tensor", "cT")
                                    wait("vector", "cT", mt)

                                    def f(v, t=t, pslot=pslot):
                                        m = q_f0.ap()[:, 0:DIM]
                                        v.tensor_tensor(m, pmm.ap()[:, pslot, :],
                                                        lnab_sb.ap()[:, 5, :], ALU.add)
                                        v.drain()
                                        v.tensor_tensor(h_sb.ap()[:, t, :],
                                                        h_sb.ap()[:, t, :], m, ALU.add)
                                    run("vector", f)
                                    mark("vector", "cV")
                            mv_sub = mark("vector", "cV")
                            if debug:
                                wait("gpsimd", "cV", mv_sub)
                                for t in range(NT):
                                    dma("gpsimd", "cST",
                                        dbg["d_hs"][sub][128 * t:128 * (t + 1), :],
                                        h_sb.ap()[:, t, :])

                        barrier_all()
                        w_lnl = dma("sync", "cL",
                                    q_f0.ap()[:, 0:1024].rearrange("p (a d) -> p a d", d=DIM),
                                    lnl_in.rearrange("a p d -> p a d"))
                        wait("vector", "cL", w_lnl)
                        run("vector", lambda v: v.tensor_copy(
                            lnl_sb.ap(),
                            q_f0.ap()[:, 0:1024].rearrange("p (a d) -> p a d", d=DIM)))
                        mv_lnl = mark("vector", "cV")
                        wait("sync", "cV", mv_lnl)
                        for half in range(2):
                            w1 = dma("sync", "cL", q_f0.ap()[:, 0:2048],
                                     boutT_in[:, 2048 * half:2048 * (half + 1)])
                            wait("vector", "cL", w1)
                            run("vector", lambda v, half=half: v.tensor_copy(
                                zTh.ap()[:, 8:16, :].rearrange("p a b -> p (a b)")[:, 2048 * half:2048 * (half + 1)],
                                q_f0.ap()[:, 0:2048]))
                            mv = mark("vector", "cV")
                            wait("sync", "cV", mv)
                        for t in range(NT):
                            buf = t % 2
                            msl = ln_tile(h_sb.ap()[:, t, :], yrow.ap()[:, buf, :],
                                          q_f0.ap()[:, 0:DIM], DIM)
                            wait("vector", "cS", msl)

                            def f(v, buf=buf):
                                yr = yrow.ap()[:, buf, :]
                                v.tensor_tensor(yr, yr, lnl_sb.ap()[:, 0, :], ALU.mult)
                                v.drain()
                                v.tensor_tensor(yr, yr, lnl_sb.ap()[:, 1, :], ALU.add)
                            run("vector", f)
                            mvx = mark("vector", "cV")
                            wait("tensor", "cV", mvx)
                            for g in range(4):
                                pp = [ptr0, ptr1][g % 2]
                                if g >= 2:
                                    wait("tensor", "cV2", cnt["cV2"] - 1)
                                run("tensor", lambda te, buf=buf, g=g, pp=pp: transp(
                                    te, pp.ap()[:, 0:128],
                                    yrow.ap()[:, buf, 128 * g:128 * (g + 1)]))
                                mt = mark("tensor", "cT2")
                                wait("vector", "cT2", mt)
                                run("vector", lambda v, t=t, g=g, pp=pp: v.tensor_copy(
                                    yT.ap()[:, g, 128 * t:128 * (t + 1)], pp.ap()[:, 0:128]))
                                mark("vector", "cV2")
                        mv_hlT = mark("vector", "cV2")
                        wait("tensor", "cV2", mv_hlT)
                        wout_mm_marks = {}
                        for oc in range(8):
                            wbuf = oc % 2
                            och = "cLo0" if wbuf == 0 else "cLo1"
                            if oc >= 2:
                                wait("sync", "cT", wout_mm_marks[oc - 2])
                            wwo = 0
                            for g in range(4):
                                wwo = dma("sync", och, zTh.ap()[:, 4 * wbuf + g, :],
                                          wout_bf[128 * g:128 * (g + 1),
                                                  512 * oc:512 * (oc + 1)])
                            wait("tensor", och, wwo)
                            for t in range(NT):
                                pslot = t % 2
                                if oc * NT + t >= 2:
                                    wait("tensor", "cV", cnt["cV"])

                                def f(te, oc=oc, t=t, wbuf=wbuf, pslot=pslot):
                                    for g in range(4):
                                        _li = te.matmul(pmm.ap()[:, pslot, :],
                                                  yT.ap()[:, g, 128 * t:128 * (t + 1)],
                                                  zTh.ap()[:, 4 * wbuf + g, :],
                                                  start=(g == 0), stop=(g == 3))
                                    return _li
                                run("tensor", f)
                                mt = mark("tensor", "cT")
                                wait("vector", "cT", mt)
                                wait("vector", "cST", cnt["cST"])

                                def f(v, oc=oc, pslot=pslot):
                                    v.tensor_tensor(oevac.ap()[:, pslot, :],
                                                    pmm.ap()[:, pslot, :],
                                                    zTh.ap()[:, 8:16, :].rearrange("p a b -> p (a b)")[:, 512 * oc:512 * (oc + 1)],
                                                    ALU.add)
                                run("vector", f)
                                mv = mark("vector", "cV")
                                wait("gpsimd", "cV", mv)
                                dma("gpsimd", "cST",
                                    out_hbm[128 * t:128 * (t + 1), 512 * oc:512 * (oc + 1)],
                                    oevac.ap()[:, pslot, :], accum=True)
                            wout_mm_marks[oc] = cnt["cT"]

                if stage >= 4:
                    with contextlib.ExitStack() as Hs:
                        barrier_all()
                        wslab8v = sbt(Hs, "wslab8", (128, 2, NDC, 512), BF16)
                        sg8 = sbt(Hs, "sg8", (128, NT, 512), BF16)
                        hrow8 = sbt(Hs, "hrow8", (128, 512), BF16)
                        hqrow8 = sbt(Hs, "hqrow8", (128, 512), BF16)
                        q_bm8v = sbt(Hs, "q_bm8", (128, 3, 256), F32)
                        q_t08v = sbt(Hs, "q_t08", (128, 2048), BF16)
                        q_t18v = sbt(Hs, "q_t18", (128, 2048), BF16)
                        q_t28v = sbt(Hs, "q_t28", (128, 2048), BF16)
                        q_t38v = sbt(Hs, "q_t38", (128, 2048), BF16)

                        def mkqs8():
                            return {"bm": q_bm8v.ap()[:, 0, :], "rb": q_bm8v.ap()[:, 1, :],
                                    "sbn": q_bm8v.ap()[:, 2, :], "t0": q_t08v.ap(),
                                    "t1": q_t18v.ap(), "t2": q_t28v.ap(), "t3": q_t38v.ap(),
                                    "f0": q_f0.ap()}

                        wait("sync", "cCC", cc_wg)
                        jcs = []
                        for r in range(NC):
                            for (f0_, w_) in [(0, 512), (512, 512), (1024, 352)]:
                                jcs.append((r, f0_, w_))
                        slab_mm_done = {}
                        slab_v_done = {}
                        for jci, (r, f0_, w_) in enumerate(jcs):
                            jg0 = FSH * r + f0_
                            bufg = (2 * jci) % 2
                            bufu = 1 - bufg
                            if jci == 0:
                                wait("sync", "cCC", cc_wu)
                            if jci >= 1:
                                wait("sync", "cT", slab_mm_done[jci - 1])
                                wait("sync", "cV", slab_v_done[jci - 1])
                            wlg = 0
                            for i in range(NDC):
                                wlg = dma("sync", "cLg", wslab8v.ap()[:, 0, i, 0:w_],
                                          wgT_ag[r, 128 * i:128 * (i + 1), f0_:f0_ + w_])
                            wlu = 0
                            for i in range(NDC):
                                wlu = dma("sync", "cLu", wslab8v.ap()[:, 1, i, 0:w_],
                                          wuT_ag[r, 128 * i:128 * (i + 1), f0_:f0_ + w_])
                            wait("tensor", "cLg", wlg)
                            for t in range(NT):
                                pslot = t % 2
                                if t >= 2:
                                    wait("tensor", "cS", cnt["cS"] - 1)

                                def f(te, t=t, w_=w_, pslot=pslot):
                                    for kc in range(NDC):
                                        _li = te.matmul(pmm.ap()[:, pslot, 0:w_],
                                                  bigT.ap()[:, kc, 128 * t:128 * (t + 1)],
                                                  wslab8v.ap()[:, 0, kc, 0:w_],
                                                  start=(kc == 0), stop=(kc == NDC - 1))
                                    return _li
                                run("tensor", f)
                                mt = mark("tensor", "cT")
                                wait("scalar", "cT", mt)
                                run("scalar", lambda s, t=t, w_=w_, pslot=pslot:
                                    s.activation(sg8.ap()[:, t, 0:w_],
                                                 pmm.ap()[:, pslot, 0:w_], ACT.Silu))
                                mark("scalar", "cS")
                            ms_g = mark("scalar", "cS")
                            wait("tensor", "cLu", wlu)
                            for t in range(NT):
                                pslot = 2 + t % 2
                                if t >= 2:
                                    wait("tensor", "cV", cnt["cV"])

                                def f(te, t=t, w_=w_, pslot=pslot):
                                    for kc in range(NDC):
                                        _li = te.matmul(pmm.ap()[:, pslot, 0:w_],
                                                  bigT.ap()[:, kc, 128 * t:128 * (t + 1)],
                                                  wslab8v.ap()[:, 1, kc, 0:w_],
                                                  start=(kc == 0), stop=(kc == NDC - 1))
                                    return _li
                                run("tensor", f)
                                mt = mark("tensor", "cT")
                                if t == 0:
                                    wait("vector", "cS", ms_g)
                                wait("vector", "cT", mt)

                                def f(v, t=t, w_=w_, pslot=pslot):
                                    v.tensor_tensor(hrow8.ap()[:, 0:w_], sg8.ap()[:, t, 0:w_],
                                                    pmm.ap()[:, pslot, 0:w_], ALU.mult)
                                    pt = 128 if (jci == len(jcs) - 1) else 0
                                    qdq_chain_f(v, hrow8.ap()[:, 0:w_], hqrow8.ap()[:, 0:w_],
                                                w_ - pt, pt, mkqs8())
                                run("vector", f)
                                mvh = mark("vector", "cV")
                                wait("tensor", "cV", mvh)
                                nb128 = (w_ + 127) // 128
                                for bi2 in range(nb128):
                                    bw = min(128, w_ - bi2 * 128)
                                    pp = [ptr0, ptr1][bi2 % 2]
                                    if bi2 >= 2:
                                        wait("tensor", "cV2", cnt["cV2"] - 1)
                                    run("tensor", lambda te, bi2=bi2, bw=bw, pp=pp: transp(
                                        te, pp.ap()[0:bw, 0:128],
                                        hqrow8.ap()[:, 128 * bi2:128 * bi2 + bw]))
                                    mt2 = mark("tensor", "cT2")
                                    wait("vector", "cT2", mt2)
                                    run("vector", lambda v, bi2=bi2, bw=bw, pp=pp:
                                        v.tensor_copy(tevac.ap()[0:bw, bi2 % 8, :],
                                                      pp.ap()[0:bw, 0:128]))
                                    mv2 = mark("vector", "cV2")
                                    wait("gpsimd", "cV2", mv2)
                                    dma("gpsimd", "cST",
                                        hqT_hbm[jg0 + 128 * bi2:jg0 + 128 * bi2 + bw,
                                                128 * t:128 * (t + 1)],
                                        tevac.ap()[0:bw, bi2 % 8, :])
                                wait("vector", "cST", cnt["cST"])
                            slab_mm_done[jci] = cnt["cT"]
                            slab_v_done[jci] = cnt["cV"]
                        st_hq = cnt["cST"]

                if stage >= 5:
                    with contextlib.ExitStack() as Is:
                        barrier_all()
                        hqTg = sbt(Is, "hqTg", (128, 86, 512), BF16)
                        wdt = sbt(Is, "wdt", (128, 4, 512), BF16)
                        sclTb = sbt(Is, "sclTb", (128, D), BF16)
                        for half in range(2):
                            w1 = dma("sync", "cL", q_f0.ap()[:, 0:2048],
                                     sclT_in[:, 2048 * half:2048 * (half + 1)])
                            wait("vector", "cL", w1)
                            run("vector", lambda v, half=half: v.tensor_copy(
                                sclTb.ap()[:, 2048 * half:2048 * (half + 1)],
                                q_f0.ap()[:, 0:2048]))
                            mv = mark("vector", "cV")
                            wait("sync", "cV", mv)
                        wait("sync", "cCC", cc_wd)
                        wait("sync", "cST", st_hq)
                        for g2 in range(2):
                            wh = 0
                            for jc in range(86):
                                wh = dma("sync", "cLh", hqTg.ap()[:, jc, :],
                                         hqT_hbm[128 * jc:128 * (jc + 1),
                                                 512 * g2:512 * (g2 + 1)])
                            for dc in range(8):
                                for jc in range(86):
                                    wslot = jc % 4
                                    wdch = ["cLd0", "cLd1", "cLd2", "cLd3"][wslot]
                                    if jc >= 4:
                                        wait("sync", "cT", cnt["cT"] - 3)
                                    wd_l = dma("sync", wdch, wdt.ap()[:, wslot, :],
                                               wdT_ag[dc, 128 * jc:128 * (jc + 1), :])
                                    wait("tensor", wdch, wd_l)
                                    wait("tensor", "cLh", wh)
                                    if dc == 0 and jc == 0:
                                        wait("tensor", "cV", cnt["cV"])

                                    def f(te, g2=g2, jc=jc, wslot=wslot):
                                        for t4 in range(4):
                                            _li = te.matmul(pmm.ap()[:, t4, :],
                                                      hqTg.ap()[:, jc,
                                                                128 * t4:128 * (t4 + 1)],
                                                      wdt.ap()[:, wslot, :],
                                                      start=(jc == 0),
                                                      stop=(jc == 85))
                                        return _li
                                    run("tensor", f)
                                    mark("tensor", "cT")
                                mt_dc = cnt["cT"]
                                wait("vector", "cT", mt_dc)
                                wait("vector", "cST", cnt["cST"])
                                for t4 in range(4):
                                    t = 4 * g2 + t4

                                    def f(v, t4=t4, dc=dc):
                                        v.tensor_tensor(oevac.ap()[:, t4, :],
                                                        pmm.ap()[:, t4, :],
                                                        sclTb.ap()[:, 512 * dc:512 * (dc + 1)],
                                                        ALU.mult)
                                    run("vector", f)
                                    mv = mark("vector", "cV")
                                    wait("gpsimd", "cV", mv)
                                    dma("gpsimd", "cST",
                                        out_hbm[128 * t:128 * (t + 1),
                                                512 * dc:512 * (dc + 1)],
                                        oevac.ap()[:, t4, :], accum=True)
                                wait("tensor", "cV", cnt["cV"])

                wait("gpsimd", "cST", cnt["cST"])
                wait("sync", "cST", cnt["cST"])

            for name in ENGS:
                fns = Q[name]

                def body(eng, fns=fns):
                    for fn in fns:
                        fn(eng)
                getattr(block, name)(body)

    return nc, dbg


_ncache = {}


def kernel(**inputs):
    up = np.asarray(inputs["L1_ffn_up_reorder_index"])
    dn = np.asarray(inputs["L1_ffn_down_reorder_index"])
    inv_up = np.argsort(up)

    x = np.ascontiguousarray(np.asarray(inputs["L0_attn_x"], np.float32)
                             .reshape(T, D)[:, up])
    wT = np.ascontiguousarray(np.asarray(inputs["L0_attn_weight"], np.float32)
                              [np.ix_(up, up)].T)
    wbT = np.ascontiguousarray(np.asarray(inputs["L0_ffn_W_base"], np.float32)
                               [np.ix_(up, up)].T)
    ftm = _f_matrix_T_perm(up).astype(bfdt)
    enc_cat = np.concatenate([np.asarray(inputs["L0_ffn_enc0"], np.float32),
                              np.asarray(inputs["L0_ffn_enc1"], np.float32)], 0)
    encT = np.ascontiguousarray(enc_cat[:, up].T)
    bias0 = np.asarray(inputs["L0_attn_bias"], np.float32)[up]
    bbase = np.asarray(inputs["L0_ffn_b_base"], np.float32)[up]
    biasv = np.ascontiguousarray(np.stack([
        np.broadcast_to(bias0, (128, D)), np.broadcast_to(bbase, (128, D))]))
    winT = np.ascontiguousarray(np.asarray(inputs["L1_attn_Win"], np.float32)[:, up].T)
    binT = np.ascontiguousarray(np.broadcast_to(
        np.asarray(inputs["L1_attn_bin"], np.float32), (128, DIM)))
    lnab = np.ascontiguousarray(np.stack([
        np.stack([np.broadcast_to(np.asarray(inputs[k], np.float32)[s], (128, DIM))
                  for k in ["L1_attn_ln1_w", "L1_attn_ln1_b", "L1_attn_ln2_w",
                            "L1_attn_ln2_b", "L1_attn_sc_lin", "L1_attn_b2"]])
        for s in range(NSUB)]))
    b1T = np.ascontiguousarray(np.asarray(inputs["L1_attn_b1"], np.float32)
                               .reshape(NSUB, DFF // 128, 128).transpose(0, 2, 1))
    phz = np.ascontiguousarray(np.stack([
        np.stack([np.asarray(inputs[k], np.float32)[s].reshape(4, 128).T
                  for k in ["L1_attn_phz_re", "L1_attn_phz_im",
                            "L1_attn_lc_re", "L1_attn_lc_im"]])
        for s in range(NSUB)]))
    lnl = np.ascontiguousarray(np.stack([
        np.broadcast_to(np.asarray(inputs["L1_attn_lnl_w"], np.float32), (128, DIM)),
        np.broadcast_to(np.asarray(inputs["L1_attn_lnl_b"], np.float32), (128, DIM))]))
    w1T = np.ascontiguousarray(np.asarray(inputs["L1_attn_W1"], np.float32)
                               .transpose(0, 2, 1))
    w2T = np.ascontiguousarray(np.asarray(inputs["L1_attn_W2"], np.float32)
                               .transpose(0, 2, 1))
    woutT = np.ascontiguousarray(np.asarray(inputs["L1_attn_Wout"], np.float32)[up].T)
    boutT = np.ascontiguousarray(np.broadcast_to(
        np.asarray(inputs["L1_attn_bout"], np.float32)[up], (128, D)))
    sclT = np.ascontiguousarray(np.broadcast_to(
        np.asarray(inputs["L1_ffn_mlp_output_scale"], np.float32)[up], (128, D)))
    lrA = np.ascontiguousarray(np.asarray(inputs["L1_ffn_lr_A"], np.float32)[up])
    lrB = np.ascontiguousarray(np.asarray(inputs["L1_ffn_lr_B"], np.float32)[:, up])
    wg_all = np.ascontiguousarray(
        np.asarray(inputs["L1_ffn_w_gate"], np.float32)[np.ix_(dn, up)])
    wu_all = np.ascontiguousarray(
        np.asarray(inputs["L1_ffn_w_up"], np.float32)[np.ix_(dn, up)])
    wd_all = np.ascontiguousarray(
        np.asarray(inputs["L1_ffn_w_down"], np.float32)[np.ix_(up, dn)])
    c0 = np.asarray(inputs["L0_ffn_coefs0"], np.float32)
    c1 = np.asarray(inputs["L0_ffn_coefs1"], np.float32)
    m0 = np.asarray(inputs["L0_ffn_mask0"]).astype(np.uint8)
    m1 = np.asarray(inputs["L0_ffn_mask1"]).astype(np.uint8)
    ident = np.eye(128, dtype=bfdt)

    in_maps = []
    for c in range(NC):
        half = c % 2
        csel = np.zeros(16, np.float32)
        if half == 0:
            csel[0] = 1.0
        else:
            csel[1] = 1.0
            csel[2 + (c - 1)] = 1.0
        in_maps.append({
            "x": x[TOK * c:TOK * (c + 1)],
            "wcTs": np.ascontiguousarray(
                np.stack([wT[:, DSH * c:DSH * (c + 1)], wbT[:, DSH * c:DSH * (c + 1)]])),
            "ften": np.ascontiguousarray(ftm[:, DSH * c:DSH * (c + 1)]),
            "c0": c0, "m0": m0, "c1": c1, "m1": m1,
            "encT": encT, "biasv": biasv, "winT": winT, "binT": binT,
            "lnab": lnab, "b1T": b1T, "phz": phz, "lnl": lnl,
            "w1T": w1T, "w2T": w2T, "woutT": woutT, "boutT": boutT,
            "sclT": sclT, "lrA": lrA, "lrB": lrB,
            "wg": np.ascontiguousarray(wg_all[FSH * c:FSH * (c + 1)]),
            "wu": np.ascontiguousarray(wu_all[FSH * c:FSH * (c + 1)]),
            "wd": np.ascontiguousarray(wd_all[DSH * c:DSH * (c + 1)]),
            "ident": ident,
            "csel": np.ascontiguousarray(np.broadcast_to(csel, (128, 16))),
        })

    import os
    stage = int(os.environ.get("KSTAGE", "5"))
    dbgf = os.environ.get("KDEBUG", "0") == "1"
    sub = int(os.environ.get("KSUB", "9"))
    key = ("prog", stage, dbgf, sub, os.environ.get("KP3", "3"), os.environ.get("KWQ", "3"))
    if key not in _ncache:
        _ncache[key] = build(debug=dbgf, stage=stage, sub=sub)
    nc_prog, _ = _ncache[key]
    res = run_bass_kernel_spmd(nc_prog, in_maps, core_ids=list(range(NC)))
    out = np.concatenate([np.asarray(r["out"], np.float32) for r in res.results], 0)
    return np.ascontiguousarray(out[:, inv_up].reshape(B_, S_, D))
